# version 17
# baseline (speedup 1.0000x reference)
import contextlib
import numpy as np
import concourse.bass as bass
import concourse.mybir as mybir
from concourse.bass_utils import run_bass_kernel_spmd

F32 = mybir.dt.float32
AF = mybir.ActivationFunctionType
ALU = mybir.AluOpType
AX = mybir.AxisListType.X

NCORE = 8
D = 1024
TOK = 2048
KC = 8
NCH = 4
CH = 512
TC = 256
NE = 32
HE = 128
TAPS = 40
NTAU = 4


def build_program(debug=False, pipe_hooks=True, pipe_fill=True, debug2=False):
    nc = bass.Bass("TRN2", target_bir_lowering=False)

    d_x = nc.dram_tensor("xin", [NCH, 128, KC * CH], F32, kind="ExternalInput").ap()
    d_ew = nc.dram_tensor("ew", [NE, 128, 3072], F32, kind="ExternalInput").ap()
    d_sg = nc.dram_tensor("sgT", [128, 4096], F32, kind="ExternalInput").ap()
    d_su = nc.dram_tensor("suT", [128, 4096], F32, kind="ExternalInput").ap()
    d_sd = nc.dram_tensor("sdT", [128, 4096], F32, kind="ExternalInput").ap()
    d_rt = nc.dram_tensor("rtT", [128, 256], F32, kind="ExternalInput").ap()
    d_rb = nc.dram_tensor("rbr", [1, 32], F32, kind="ExternalInput").ap()
    d_wm = nc.dram_tensor("wmods", [KC, 128, 6144], F32, kind="ExternalInput").ap()
    d_aa = nc.dram_tensor("A_all", [128, 128], F32, kind="ExternalInput").ap()
    d_bb = nc.dram_tensor("B_pp", [128, 128], F32, kind="ExternalInput").ap()
    d_cc = nc.dram_tensor("C_pp", [128, 128], F32, kind="ExternalInput").ap()
    d_dd = nc.dram_tensor("D_pp", [128, 8], F32, kind="ExternalInput").ap()
    d_id = nc.dram_tensor("ident", [128, 128], F32, kind="ExternalInput").ap()
    d_ct = nc.dram_tensor("cT", [128, 8], F32, kind="ExternalInput").ap()
    d_y = nc.dram_tensor("yout", [NCH, 128, KC * CH], F32, kind="ExternalOutput").ap()
    d_wfd = nc.dram_tensor("wfd", [32, TOK], F32, kind="ExternalOutput").ap()
    if debug2:
        d2_h1 = nc.dram_tensor("d2_h1", [128, KC * 576], F32, kind="ExternalOutput").ap()
        d2_h2 = nc.dram_tensor("d2_h2", [128, KC * CH], F32, kind="ExternalOutput").ap()
        d2_wf = nc.dram_tensor("d2_wf", [32, TOK], F32, kind="ExternalOutput").ap()
        d2_wb = nc.dram_tensor("d2_wb", [128, 8 * TC], F32, kind="ExternalOutput").ap()
        d2_hsb = nc.dram_tensor("d2_hsb", [128, 2048], F32, kind="ExternalOutput").ap()
        d2_modv = nc.dram_tensor("d2_modv", [128, 48], F32, kind="ExternalOutput").ap()
        d2_modt = nc.dram_tensor("d2_modt", [128, 48], F32, kind="ExternalOutput").ap()
        d2_silc = nc.dram_tensor("d2_silc", [128, 8], F32, kind="ExternalOutput").ap()
        d2_kt = nc.dram_tensor("d2_kt", [128, TAPS * 8], F32, kind="ExternalOutput").ap()
        d2_ctv = nc.dram_tensor("d2_ctv", [128, 8], F32, kind="ExternalOutput").ap()
    if debug:
        d_dbg_h1 = nc.dram_tensor("dbg_h1", [NCH, 128, KC * 576], F32, kind="ExternalOutput").ap()
        d_dbg_x1 = nc.dram_tensor("dbg_x1", [NCH, 128, KC * CH], F32, kind="ExternalOutput").ap()
        d_dbg_h2 = nc.dram_tensor("dbg_h2", [NCH, 128, KC * CH], F32, kind="ExternalOutput").ap()
        d_dbg_wf = nc.dram_tensor("dbg_wf", [32, TOK], F32, kind="ExternalOutput").ap()
        d_dbg_kt = nc.dram_tensor("dbg_kt", [128, TAPS * 8], F32, kind="ExternalOutput").ap()
        d_dbg_mod = nc.dram_tensor("dbg_mod", [128, 48], F32, kind="ExternalOutput").ap()

    es = contextlib.ExitStack()
    sb = lambda name, shape: es.enter_context(nc.sbuf_tensor(name, shape, F32))

    xs = sb("xs", [128, 2 * KC * CH])
    h1r = sb("h1r", [128, KC * 576])
    h2 = sb("h2", [128, KC * CH])
    gup = sb("gup", [128, 2 * 2048])
    dnb = sb("dnb", [128, 3 * 1024])
    sgT = sb("sgTb", [128, 4096])
    suT = sb("suTb", [128, 4096])
    sdT = sb("sdTb", [128, 4096])
    rtT = sb("rtTb", [128, 256])
    rbr = sb("rbrb", [1, 32])
    hsb = sb("hsb", [128, 2048])
    aal = hsb[:, 0:128]
    bpp = hsb[:, 128:256]
    cpp = hsb[:, 256:384]
    wpos = hsb[:, 384:512]
    wneg = hsb[:, 512:640]
    pw = hsb[:, 640:768]
    pbuf = hsb[:, 768:1024]
    dpp = sb("dpp", [128, 8])
    ktab = sb("ktab", [128, TAPS * 8])
    tcolv = sb("tcolv", [128, TAPS])
    ctv = sb("ctv", [128, 8])
    siluc = sb("siluc", [128, 8])
    modv = sb("modv", [128, 48])
    modt = sb("modt", [128, 48])
    idn = sb("idn", [128, 128])
    cst = sb("cst", [128, 4])
    ones1 = es.enter_context(nc.sbuf_tensor("ones1", [1, 128], F32))
    onesc = sb("onesc", [128, 1])
    rows = es.enter_context(nc.sbuf_tensor("rows", [1, 2048], F32))
    R_MEAN, R_Q, R_T, R_R = 0, 512, 1024, 1536
    R_MR = R_Q
    xsq2 = sb("xsq2", [128, 2 * CH])
    tA = sb("tA", [128, 256])
    tB = sb("tB", [128, 256])
    yacc = sb("yacc", [128, CH])
    sE = sb("sE", [128, 2 * TC])
    tE = sb("tE", [128, TC])
    hEb = sb("hEb", [128, 4 * TC])
    ssh = sb("ssh", [128, CH])
    s0n = sb("s0n", [128, 32])
    s0e = sb("s0e", [128, 32])
    swb = sb("swb", [128, 32])
    mskb = sb("mskb", [128, 32])
    maccb = sb("maccb", [128, 32])
    wTb = sb("wTb", [128, 32])
    r1 = sb("r1", [128, 8])
    wF = es.enter_context(nc.sbuf_tensor("wF", [32, TOK], F32))
    wbS = sb("wbS", [128, 8 * TC])

    pb = [es.enter_context(nc.psum_tensor(f"pb{i}", [128, 512], F32)) for i in range(8)]

    sem_names = ["tM", "vV", "sS", "Eblk", "Ewm0", "Ewm1", "Exs0", "Exs1",
                 "Egu0", "Egu1", "Edn0", "Edn1", "Edn2", "Ebnc",
                 "Ewb0", "Ewb1", "Ewb2", "Ewb3", "Ewb4", "Ewb5", "Ewb6", "Ewb7",
                 "Edbg"]
    sems = {n: es.enter_context(nc.semaphore(n)) for n in sem_names}
    cnt = {n: 0 for n in sem_names}
    queues = {"sync": [], "tensor": [], "vector": [], "scalar": [], "gpsimd": []}

    def E(q, fn, waits=(), inc=None):
        w = tuple((s, v) for (s, v) in waits if v > 0)
        queues[q].append((fn, w, inc))
        if inc is not None:
            cnt[inc[0]] += inc[1]
        return cnt[inc[0]] if inc is not None else None

    spc = cst[:, 2:4]

    def SP():
        E("vector", lambda e: e.tensor_copy(spc[:, 0:1], spc[:, 1:2]))

    def EVS(fn, waits=()):
        w = list(waits) + [("vV", cnt["vV"])]
        return E("vector", fn, waits=w, inc=("vV", 1))

    SGC = dict(skip_group_check=True)

    def BARRIER():
        t1 = E("tensor", lambda e, a=cnt["vV"], b=cnt["sS"]: e.matmul(
            pb[7][0:1, 0:1], onesc[:, :], onesc[:, :],
            start=True, stop=True, **SGC),
            waits=[("vV", cnt["vV"]), ("sS", cnt["sS"])], inc=("tM", 1))
        E("vector", lambda e: e.tensor_copy(spc[:, 0:1], spc[:, 1:2]),
          waits=[("tM", t1)], inc=("vV", 1))
        E("scalar", lambda e: e.copy(spc[:, 0:1], spc[:, 1:2]),
          waits=[("tM", t1)], inc=("sS", 1))
        ww = [("vV", cnt["vV"]), ("sS", cnt["sS"])]
        E("tensor", lambda e: e.matmul(pb[7][0:1, 0:1], onesc[:, :], onesc[:, :],
                                       start=True, stop=True, **SGC), waits=ww)

    rec = {
        "res1": [0] * NCH,
        "ln2_apply": [0] * NCH,
        "wf": [0] * NCH,
        "evac_dn": [0] * (2 * NCH),
        "evac_sh": [0] * NCH,
        "wslot": {},
        "gu_free": {},
        "hslot": {},
        "hEv": {},
        "stat_b": [0, 0],
        "pbg": {b: [] for b in range(8)},
    }

    def bank_w(b):
        return list(rec["pbg"][b])

    def bank_r(b, sem, val):
        rec["pbg"][b] = [(sem, val)]

    def dg_wb(i, half, e_):
        g = (2 * i + half) * NE + e_
        return ("Ewb%d" % (g % 8), 16 * (g // 8 + 1))



    def qdma(sem, dst, src, waits=()):
        E("scalar", lambda e, d=dst, s=src: e.dma_start(out=d, in_=s),
          waits=waits, inc=(sem, 16))
        return cnt[sem]
    for dst, src in [(idn[:, :], d_id), (ctv[:, :], d_ct),
                     (aal[:, :], d_aa), (bpp[:, :], d_bb), (cpp[:, :], d_cc),
                     (dpp[:, :], d_dd), (rtT[:, :], d_rt), (rbr[:, :], d_rb),
                     (sgT[:, :], d_sg), (suT[:, :], d_su), (sdT[:, :], d_sd)]:
        qdma("Eblk", dst, src)
    BLK = cnt["Eblk"]
    dq_x = [0] * NCH
    dq_x[0] = qdma("Exs0", xs[:, 0:KC * CH], d_x[0])
    dq_x[1] = qdma("Exs1", xs[:, KC * CH:2 * KC * CH], d_x[1])

    E("vector", lambda e: e.memset(ones1[:, :], 1.0))
    E("vector", lambda e: e.memset(onesc[:, :], 1.0))
    E("vector", lambda e: e.memset(cst[:, 0:1], 1e-5))
    for t in range(TAPS):
        E("vector", lambda e, t=t: e.memset(tcolv[:, t:t + 1], float(t)))
    for k in range(KC):
        E("vector", lambda e, k=k: e.memset(h1r[:, k * 576:k * 576 + 64], 0.0))
    E("vector", lambda e: e.tensor_mul(wpos[:, :], cpp[:, :], bpp[:, :]),
      waits=[("Eblk", BLK)])
    v_w = E("vector", lambda e: e.tensor_scalar_mul(wneg[:, :], wpos[:, :], -1.0),
            inc=("vV", 1))

    s_silu = E("scalar", lambda e: e.activation(siluc[:, :], ctv[:, :], AF.Silu),
               waits=[("Eblk", BLK)], inc=("sS", 1))
    v_pw = {}
    s_p = {}
    for t in range(TAPS):
        slot = (t % 2) * 128
        w_p = [("vV", v_w), ("Eblk", BLK)]
        if t >= 2:
            w_p.append(("vV", v_pw[t - 2]))
        s_p[t] = E("scalar",
                   lambda e, t=t, sl=slot: e.activation(
                       pbuf[:, sl:sl + 128], aal[:, :], AF.Exp,
                       scale=tcolv[:, t:t + 1]),
                   waits=w_p, inc=("sS", 1))
        wsrc = wpos if t % 2 == 0 else wneg
        EVS(lambda e, sl=slot, ws=wsrc: e.tensor_mul(
            pw[:, :], pbuf[:, sl:sl + 128], ws[:, :]),
            waits=[("sS", s_p[t])])
        v_pw[t] = EVS(lambda e, t=t: e.reduce_sum(
            ktab[:, t * 8:(t + 1) * 8],
            pw.rearrange("p (a c) -> p a c", a=8), axis=AX))
    v_kd = E("vector", lambda e: e.tensor_add(ktab[:, 0:8], ktab[:, 0:8], dpp[:, :]),
             waits=[("vV", v_pw[0])], inc=("vV", 1))

    t_piece = [0, 0]
    t_slab = 0
    for k in range(KC):
        for p in range(3):
            n = k * 3 + p
            slot = (n % 2) * 2048
            w = [("tM", t_piece[n % 2])] if t_piece[n % 2] else []
            wmsem = "Ewm%d" % (n % 2)
            dq_slab = qdma(wmsem, gup[:, slot:slot + 2048],
                           d_wm[k][:, p * 2048:(p + 1) * 2048], waits=w)
            for jj in range(16):
                j = p * 16 + jj
                wj = [(wmsem, dq_slab), ("sS", s_silu)] if jj == 0 else []
                inc = ("tM", 1) if jj == 15 else None
                t_slab = E("tensor",
                           lambda e, k=k, j=j, jj=jj, sl=slot: e.matmul(
                               pb[4][:, j:j + 1], gup[:, sl + jj * 128: sl + (jj + 1) * 128],
                               siluc[:, k:k + 1], start=(k == 0 and j == 0),
                               stop=(k == KC - 1), **SGC),
                           waits=wj, inc=inc) or t_slab
            t_piece[n % 2] = t_slab
    t_mod = t_slab
    v_mev = EVS(lambda e: e.tensor_copy(modv[:, :], pb[4][:, 0:48]),
                waits=[("tM", t_mod)])
    bank_r(4, "vV", v_mev)
    EVS(lambda e: e.tensor_copy(modt[:, 0:8], modv[:, 0:8]))
    EVS(lambda e: e.tensor_scalar_add(modt[:, 8:16], modv[:, 8:16], 1.0))
    EVS(lambda e: e.tensor_copy(modt[:, 24:32], modv[:, 24:32]))
    EVS(lambda e: e.tensor_scalar_add(modt[:, 32:40], modv[:, 32:40], 1.0))
    v_mod = 0
    for (src0, dst0) in [(16, 16), (40, 40)]:
        EVS(lambda e, s=src0: e.tensor_scalar_mul(
            tA[:, 0:8], modv[:, s:s + 8], -1.0))
        EVS(lambda e, s=src0: e.tensor_scalar(
            out=tB[:, 0:8], in0=modv[:, s:s + 8], scalar1=0.0, scalar2=None,
            op0=ALU.is_ge))
        EVS(lambda e, s=src0: e.tensor_max(
            tA[:, 0:8], tA[:, 0:8], modv[:, s:s + 8]))
        EVS(lambda e: e.tensor_scalar(
            out=tB[:, 0:8], in0=tB[:, 0:8], scalar1=2.0, scalar2=-1.0,
            op0=ALU.mult, op1=ALU.add))
        EVS(lambda e: e.tensor_scalar_max(tA[:, 0:8], tA[:, 0:8], 0.25))
        v_mod = EVS(lambda e, d=dst0: e.tensor_mul(
            modt[:, d:d + 8], tB[:, 0:8], tA[:, 0:8]))


    if debug:
        E("scalar", lambda e: e.dma_start(out=d_dbg_kt, in_=ktab[:, :]),
          waits=[("vV", v_kd)], inc=("Edbg", 16))
        E("scalar", lambda e: e.dma_start(out=d_dbg_mod, in_=modt[:, :]),
          waits=[("vV", v_mod)], inc=("Edbg", 16))

    def ln_stats(i, slot, src_guard_v):
        base = slot * KC * CH
        t_b = 0
        for k in range(KC):
            xsl = xs[:, base + k * CH: base + (k + 1) * CH]
            sq = xsq2[:, (k % 2) * CH:(k % 2 + 1) * CH]
            v_sq = E("vector", lambda e, x=xsl, q=sq: e.tensor_mul(q, x, x),
                     waits=[("tM", rec["stat_b"][k % 2])], inc=("vV", 1))
            wa = ([("vV", src_guard_v), ("Exs%d" % (i % 2), dq_x[i])] + bank_w(3)) if k == 0 else []
            E("tensor", lambda e, x=xsl, k=k: e.matmul(
                pb[3][0:1, 0:512], onesc[:, :], x,
                start=(k == 0), stop=(k == KC - 1), **SGC), waits=wa)
            t_b = E("tensor", lambda e, q=sq, k=k: e.matmul(
                pb[3][32:33, 0:512], onesc[:, :], q,
                start=False, stop=(k == KC - 1), **SGC),
                waits=[("vV", v_sq)], inc=("tM", 1))
            rec["stat_b"][k % 2] = t_b
        EVS(lambda e: e.tensor_scalar_mul(
            rows[0:1, R_MEAN:R_MEAN + 512], pb[3][0:1, 0:512], 1.0 / 1024.0),
            waits=[("tM", t_b)])
        EVS(lambda e: e.tensor_scalar_mul(
            rows[0:1, R_Q:R_Q + 512], pb[3][32:33, 0:512], 1.0 / 1024.0))
        EVS(lambda e: e.tensor_mul(
            rows[0:1, R_T:R_T + 512], rows[0:1, R_MEAN:R_MEAN + 512],
            rows[0:1, R_MEAN:R_MEAN + 512]))
        v_var = EVS(lambda e: e.tensor_sub(
            rows[0:1, R_Q:R_Q + 512], rows[0:1, R_Q:R_Q + 512],
            rows[0:1, R_T:R_T + 512]))
        s_sd = E("scalar", lambda e: e.activation(
            rows[0:1, R_T:R_T + 512], rows[0:1, R_Q:R_Q + 512], AF.Sqrt,
            bias=cst[0:1, 0:1]), waits=[("vV", v_var)], inc=("sS", 1))
        EVS(lambda e: e.reciprocal(
            rows[0:1, R_R:R_R + 512], rows[0:1, R_T:R_T + 512]),
            waits=[("sS", s_sd)])
        v_rows = EVS(lambda e: e.tensor_mul(
            rows[0:1, R_MR:R_MR + 512], rows[0:1, R_MEAN:R_MEAN + 512],
            rows[0:1, R_R:R_R + 512]))
        return v_rows

    def ln_bcast_apply(slot, half, ca_col, sa_col, dst, dst_off, v_guard):
        base = slot * KC * CH
        off = half * 256
        E("tensor", lambda e, o=off: e.matmul(
            pb[3][:, 0:256], ones1[:, :], rows[0:1, R_R + o:R_R + o + 256],
            start=True, stop=True, **SGC),
          waits=[("vV", v_guard)])
        t_bc = E("tensor", lambda e, o=off: e.matmul(
            pb[3][:, 256:512], ones1[:, :], rows[0:1, R_MR + o:R_MR + o + 256],
            start=False, stop=True, **SGC), inc=("tM", 1))
        v_apply = v_guard
        for k in range(KC):
            xsl = xs[:, base + k * CH + off: base + k * CH + off + 256]
            E("vector", lambda e, x=xsl: e.tensor_mul(tA[:, :], x, pb[3][:, 0:256]),
              waits=[("tM", t_bc)] if k == 0 else [])
            E("vector", lambda e: e.tensor_sub(tA[:, :], tA[:, :], pb[3][:, 256:512]))
            dsl = dst[:, dst_off(k, half): dst_off(k, half) + 256]
            inc = ("vV", 1) if k == KC - 1 else None
            v_apply = E("vector", lambda e, d=dsl, k=k: e.tensor_scalar(
                out=d, in0=tA[:, :], scalar1=modt[:, ca_col + k:ca_col + k + 1],
                scalar2=modt[:, sa_col + k:sa_col + k + 1],
                op0=ALU.mult, op1=ALU.add), inc=inc) or v_apply
        bank_r(3, "vV", v_apply)
        return v_apply

    def ln_phase(i, slot, ca_col, sa_col, dst, dst_off, src_guard_v):
        v_rows = ln_stats(i, slot, src_guard_v)
        v0 = ln_bcast_apply(slot, 0, ca_col, sa_col, dst, dst_off, v_rows)
        return ln_bcast_apply(slot, 1, ca_col, sa_col, dst, dst_off, v0)

    def conv_ops_list(i, slot):
        base = slot * KC * CH
        ops = []
        for k in range(KC):
            hb = k * 576
            ops.append(lambda hb=hb, k=k: E("vector", lambda e: e.tensor_scalar_mul(
                yacc[:, :], h1r[:, hb + 64:hb + 64 + CH], ktab[:, k:k + 1])))
            for t in range(1, TAPS):
                ops.append(lambda hb=hb, t=t, k=k: E("vector", lambda e: e.scalar_tensor_tensor(
                    out=yacc[:, :], in0=h1r[:, hb + 64 - t:hb + 64 - t + CH],
                    scalar=ktab[:, t * 8 + k:t * 8 + k + 1], in1=yacc[:, :],
                    op0=ALU.mult, op1=ALU.add)))
            xsl = xs[:, base + k * CH: base + (k + 1) * CH]
            inc = ("vV", 1) if k == KC - 1 else None
            ops.append(lambda x=xsl, k=k, inc=inc, i=i: rec["res1"].__setitem__(
                i, E("vector", lambda e: e.scalar_tensor_tensor(
                    out=x, in0=yacc[:, :], scalar=modt[:, 16 + k:17 + k], in1=x,
                    op0=ALU.mult, op1=ALU.add), inc=inc) or rec["res1"][i]))
        return ops

    def conv_phase(i, slot):
        for op in conv_ops_list(i, slot):
            op()
        return rec["res1"][i]

    def halo_phase():
        for k in range(KC):
            hb = k * 576
            E("vector", lambda e, hb=hb: e.tensor_copy(
                h1r[:, hb:hb + 64], h1r[:, hb + 512:hb + 576]))

    def router_phase(i):
        v_mx = 0
        s_exp = 0
        v_wf = 0
        t_tr = 0
        for tau in range(NTAU):
            for k in range(KC):
                w = []
                if k == 0:
                    w = [("vV", rec["ln2_apply"][i]), ("Eblk", BLK)]
                    if tau > 0:
                        w += [("sS", s_exp), ("vV", v_mx)]
                    else:
                        w += bank_w(2)
                E("tensor", lambda e, k=k, tau=tau: e.matmul(
                    pb[2][:, 0:32],
                    h2[:, k * CH + tau * 128: k * CH + tau * 128 + 128],
                    rtT[:, k * 32:(k + 1) * 32],
                    start=(k == 0), stop=False, **SGC), waits=w)
            t_l = E("tensor", lambda e: e.matmul(
                pb[2][:, 0:32], ones1[:, :], rbr[0:1, :],
                start=False, stop=True, **SGC), inc=("tM", 1))
            EVS(lambda e: e.reduce_max(r1[:, 0:1], pb[2][:, 0:32], axis=AX),
                waits=[("tM", t_l)])
            v_mx = EVS(lambda e: e.tensor_scalar_mul(r1[:, 1:2], r1[:, 0:1], -1.0))
            s_exp = E("scalar", lambda e: e.activation(
                s0e[:, :], pb[2][:, 0:32], AF.Exp, bias=r1[:, 1:2]),
                waits=[("vV", v_mx)], inc=("sS", 1))
            EVS(lambda e: e.reduce_sum(r1[:, 2:3], s0e[:, :], axis=AX),
                waits=[("sS", s_exp)])
            EVS(lambda e: e.reciprocal(r1[:, 3:4], r1[:, 2:3]))
            EVS(lambda e: e.tensor_scalar_mul(s0n[:, :], s0e[:, :], r1[:, 3:4]))
            EVS(lambda e: e.tensor_copy(swb[:, :], s0n[:, :]))
            for k4 in range(4):
                EVS(lambda e: e.reduce_max(r1[:, 4:5], swb[:, :], axis=AX))
                if k4 == 0:
                    EVS(lambda e: e.tensor_copy(r1[:, 5:6], r1[:, 4:5]))
                else:
                    EVS(lambda e: e.tensor_add(r1[:, 5:6], r1[:, 5:6], r1[:, 4:5]))
                EVS(lambda e: e.tensor_scalar(
                    out=mskb[:, :], in0=swb[:, :], scalar1=r1[:, 4:5], scalar2=None,
                    op0=ALU.is_ge))
                if k4 == 0:
                    EVS(lambda e: e.tensor_copy(maccb[:, :], mskb[:, :]))
                else:
                    EVS(lambda e: e.tensor_add(maccb[:, :], maccb[:, :], mskb[:, :]))
                if k4 < 3:
                    EVS(lambda e: e.scalar_tensor_tensor(
                        out=swb[:, :], in0=mskb[:, :], scalar=-4.0, in1=swb[:, :],
                        op0=ALU.mult, op1=ALU.add))
            EVS(lambda e: e.tensor_scalar(
                out=r1[:, 6:7], in0=r1[:, 5:6], scalar1=1e-12, scalar2=None,
                op0=ALU.max))
            EVS(lambda e: e.reciprocal(r1[:, 7:8], r1[:, 6:7]))
            EVS(lambda e: e.tensor_scalar_mul(s0n[:, :], s0n[:, :], r1[:, 7:8]))
            v_wt = EVS(lambda e: e.tensor_mul(wTb[:, :], s0n[:, :], maccb[:, :]))
            t_tr = E("tensor", lambda e: e.matmul(
                pb[4][0:32, 0:128], wTb[:, :], idn[:, :], is_transpose=True,
                start=True, stop=True, **SGC),
                waits=[("vV", v_wt)] + bank_w(4),
                inc=("tM", 1))
            g = i * NTAU + tau
            v_wf = E("vector", lambda e, g=g: e.tensor_copy(
                wF[0:32, g * 128:(g + 1) * 128], pb[4][0:32, 0:128]),
                waits=[("tM", t_tr)], inc=("vV", 1))
            bank_r(4, "vV", v_wf)
        rec["pbg"][2] = [("vV", v_mx), ("sS", s_exp)]
        rec["wf"][i] = v_wf
        return v_wf

    def experts_phase(i, hooks=None, filler=None):
        base = i * KC * CH
        fpop = [0]
        if filler:
            per = max(1, (len(filler) + 51) // 52)
        for half in range(2):
            tcv = 2 * i + half
            off = half * TC
            t_gate = {}
            t_up = {}
            v_he = {}

            def emit_gup(e_):
                g = tcv * NE + e_
                slot = (g % 2) * 2048
                ub = pb[1 + e_ % 2]
                for k in range(KC):
                    w = []
                    if k == 0:
                        w = [("Egu%d" % (g % 2), 16 * (g // 2 + 1))] + bank_w(0)
                    inc = ("tM", 1) if k == KC - 1 else None
                    t_gate[e_] = E("tensor", lambda e, k=k, sl=slot, o=off: e.matmul(
                        pb[0][:, 0:TC], gup[:, sl + k * 128: sl + (k + 1) * 128],
                        h2[:, k * CH + o: k * CH + o + TC],
                        start=(k == 0), stop=(k == KC - 1), **SGC),
                        waits=w, inc=inc) or t_gate.get(e_)
                for k in range(KC):
                    w = bank_w(1 + e_ % 2) if k == 0 else []
                    inc = ("tM", 1) if k == KC - 1 else None
                    t_up[e_] = E("tensor", lambda e, k=k, sl=slot, o=off, ub=ub: e.matmul(
                        ub[:, 0:TC], gup[:, sl + 1024 + k * 128: sl + 1024 + (k + 1) * 128],
                        h2[:, k * CH + o: k * CH + o + TC],
                        start=(k == 0), stop=(k == KC - 1), **SGC),
                        waits=w, inc=inc) or t_up.get(e_)
                rec["gu_free"][g] = t_up[e_]
                s_si = E("scalar", lambda e, e_=e_: e.activation(
                    sE[:, (e_ % 2) * TC:(e_ % 2 + 1) * TC], pb[0][:, 0:TC], AF.Silu),
                    waits=[("tM", t_gate[e_]), ("vV", rec.get("sE%d" % (e_ % 2), 0))],
                    inc=("sS", 1))
                bank_r(0, "sS", s_si)
                v_t = E("vector", lambda e, e_=e_, ub=ub: e.tensor_mul(
                    tE[:, :], ub[:, 0:TC], sE[:, (e_ % 2) * TC:(e_ % 2 + 1) * TC]),
                    waits=[("tM", t_up[e_]), ("sS", s_si)], inc=("vV", 1))
                bank_r(1 + e_ % 2, "vV", v_t)
                rec["sE%d" % (e_ % 2)] = v_t
                hs = (g % 4) * TC
                wh = [("tM", rec["hslot"].get(g - 4, 0)), dg_wb(i, half, e_)]
                v_he[e_] = E("vector", lambda e, hs=hs, g=g: e.tensor_mul(
                    hEb[:, hs:hs + TC], tE[:, :], wbS[:, (g % 8) * TC:(g % 8) * TC + TC]),
                    waits=wh, inc=("vV", 1))
                rec["hEv"][g] = v_he[e_]

            def emit_down(e_):
                g = tcv * NE + e_
                slot3 = (g % 3) * 1024
                hs = (g % 4) * TC
                t_d = 0
                for c in range(KC):
                    bank = pb[4 + c // 2]
                    po = (c % 2) * 256
                    w = []
                    if c == 0:
                        w = [("vV", v_he[e_]), ("Edn%d" % (g % 3), 16 * (g // 3 + 1))]
                        if e_ == 0:
                            for bnk in range(4, 8):
                                w += bank_w(bnk)
                    start = (e_ == 0 and c % 2 == 0)
                    inc = ("tM", 1) if c == KC - 1 else None
                    t_d = E("tensor", lambda e, c=c, sl=slot3, b=bank, po=po, s=start: e.matmul(
                        b[:, po:po + TC], dnb[:, sl + c * 128: sl + (c + 1) * 128],
                        hEb[:, hs:hs + TC],
                        start=s, stop=(e_ == NE - 1), **SGC),
                        waits=w, inc=inc) or t_d
                rec["wslot"][g] = t_d
                rec["hslot"][g] = t_d

            for e_ in range(NE):
                emit_gup(e_)
                if e_ >= 2:
                    emit_down(e_ - 2)
                if hooks and (half, e_) in hooks:
                    hooks[(half, e_)]()
                if filler and (half, e_) >= (0, 12):
                    n = 0
                    while fpop[0] < len(filler) and n < per:
                        filler[fpop[0]]()
                        fpop[0] += 1
                        n += 1
            emit_down(NE - 2)
            emit_down(NE - 1)
            t_last = rec["wslot"][tcv * NE + NE - 1]
            v_ev = 0
            for c in range(KC):
                bank = pb[4 + c // 2]
                po = (c % 2) * 256
                xsl = xs[:, (i % 2) * KC * CH + c * CH + off: (i % 2) * KC * CH + c * CH + off + TC]
                w = [("tM", t_last)] if c == 0 else []
                inc = ("vV", 1) if c == KC - 1 else None
                v_ev = E("vector", lambda e, x=xsl, b=bank, po=po, c=c: e.scalar_tensor_tensor(
                    out=x, in0=b[:, po:po + TC], scalar=modt[:, 40 + c:41 + c], in1=x,
                    op0=ALU.mult, op1=ALU.add), waits=w, inc=inc) or v_ev
            rec["evac_dn"][tcv] = v_ev
            for bnk in range(4, 8):
                bank_r(bnk, "vV", v_ev)
        if filler:
            while fpop[0] < len(filler):
                filler[fpop[0]]()
                fpop[0] += 1

    def shared_phase(i):
        base = (i % 2) * KC * CH
        v_hs = 0
        s_sh = {}
        for j in range(4):
            for k in range(KC):
                w = []
                if k == 0:
                    w = [("Eblk", BLK)] + bank_w(0)
                inc = ("tM", 1) if k == KC - 1 else None
                t_sg = E("tensor", lambda e, j=j, k=k: e.matmul(
                    pb[0][:, 0:512], sgT[:, (k * 4 + j) * 128:(k * 4 + j + 1) * 128],
                    h2[:, k * CH:(k + 1) * CH],
                    start=(k == 0), stop=(k == KC - 1), **SGC), waits=w, inc=inc)
                if inc:
                    t_sg_last = t_sg
            s_sh[j] = E("scalar", lambda e, j=j: e.activation(
                ssh[:, 0:CH], pb[0][:, 0:512], AF.Silu),
                waits=[("tM", t_sg_last), ("vV", rec.get("ssh_l", 0))],
                inc=("sS", 1))
            bank_r(0, "sS", s_sh[j])
            for k in range(KC):
                w = []
                if k == 0:
                    w = [("Eblk", BLK)] + bank_w(1)
                inc = ("tM", 1) if k == KC - 1 else None
                t_su = E("tensor", lambda e, j=j, k=k: e.matmul(
                    pb[1][:, 0:512], suT[:, (k * 4 + j) * 128:(k * 4 + j + 1) * 128],
                    h2[:, k * CH:(k + 1) * CH],
                    start=(k == 0), stop=(k == KC - 1), **SGC), waits=w, inc=inc)
                if inc:
                    t_su_last = t_su
            v_hs = E("vector", lambda e, j=j: e.tensor_mul(
                hsb[:, j * 512:(j + 1) * 512], pb[1][:, 0:512],
                ssh[:, 0:CH]),
                waits=[("tM", t_su_last), ("sS", s_sh[j])], inc=("vV", 1))
            bank_r(1, "vV", v_hs)
            rec["ssh_l"] = v_hs
        v_ev = 0
        for c in range(KC):
            bank = pb[2 + (c % 2)]
            t_sd = 0
            for j in range(4):
                w = []
                if j == 0:
                    w = [("Eblk", BLK), ("vV", v_hs)] + bank_w(2 + c % 2)
                inc = ("tM", 1) if j == 3 else None
                t_sd = E("tensor", lambda e, j=j, c=c, b=bank: e.matmul(
                    b[:, 0:512], sdT[:, (j * 8 + c) * 128:(j * 8 + c + 1) * 128],
                    hsb[:, j * 512:(j + 1) * 512],
                    start=(j == 0), stop=(j == 3), **SGC), waits=w, inc=inc) or t_sd
            xsl = xs[:, base + c * CH:(base + (c + 1) * CH)]
            inc = ("vV", 1)
            v_ev = E("vector", lambda e, x=xsl, b=bank, c=c: e.scalar_tensor_tensor(
                out=x, in0=b[:, 0:512], scalar=modt[:, 40 + c:41 + c], in1=x,
                op0=ALU.mult, op1=ALU.add), waits=[("tM", t_sd)], inc=inc)
            bank_r(2 + c % 2, "vV", v_ev)
        rec["evac_sh"][i] = v_ev

    def h1off(k, half):
        return k * 576 + 64 + half * 256

    def h2off(k, half):
        return k * CH + half * 256

    for i in range(NCH):
        slot = i % 2
        if i == 0:
            v_h1 = ln_phase(0, 0, 8, 0, h1r, h1off, src_guard_v=v_mod)
            if debug:
                E("scalar", lambda e: e.dma_start(out=d_dbg_h1[0], in_=h1r[:, :]),
                  waits=[("vV", v_h1)], inc=("Edbg", 16))
            conv_phase(0, 0)
        if debug:
            E("scalar", lambda e, i=i, s=slot: e.dma_start(
                out=d_dbg_x1[i], in_=xs[:, s * KC * CH:(s + 1) * KC * CH]),
              waits=[("vV", rec["res1"][i])], inc=("Edbg", 16))
        v_h2 = ln_phase(i, slot, ca_col=32, sa_col=24, dst=h2,
                        dst_off=h2off, src_guard_v=rec["res1"][i])
        rec["ln2_apply"][i] = v_h2
        if debug:
            E("scalar", lambda e, i=i: e.dma_start(out=d_dbg_h2[i], in_=h2[:, :]),
              waits=[("vV", v_h2)], inc=("Edbg", 16))
        router_phase(i)
        shared_phase(i)
        hooks = None
        fill = None
        if i + 1 < NCH and (pipe_hooks or pipe_fill):
            ns = (i + 1) % 2
            fill = conv_ops_list(i + 1, ns)

            def hook_stats(i=i, ns=ns):
                halo_phase()
                v = ln_stats(i + 1, ns, 0)
                rec["_v_rows"] = v

            def hook_b0(i=i, ns=ns):
                rec["_v_ap0"] = ln_bcast_apply(ns, 0, 8, 0, h1r, h1off,
                                               rec["_v_rows"])

            def hook_b1(i=i, ns=ns):
                ln_bcast_apply(ns, 1, 8, 0, h1r, h1off, rec["_v_ap0"])

            hooks = {(0, 2): hook_stats, (0, 6): hook_b0, (0, 10): hook_b1}
            if not pipe_fill:
                fill = None
        experts_phase(i, hooks=hooks, filler=fill)
        if i + 1 < NCH and hooks is not None and fill is None and pipe_hooks:
            conv_phase(i + 1, (i + 1) % 2)
        if i + 1 < NCH and hooks is None:
            halo_phase()
            vr = ln_stats(i + 1, (i + 1) % 2, 0)
            v0 = ln_bcast_apply((i + 1) % 2, 0, 8, 0, h1r, h1off, vr)
            ln_bcast_apply((i + 1) % 2, 1, 8, 0, h1r, h1off, v0)
            conv_phase(i + 1, (i + 1) % 2)
        xsem = "Exs%d" % slot
        dq_out_i = E("scalar", lambda e, i=i, s=slot: e.dma_start(
            out=d_y[i], in_=xs[:, s * KC * CH:(s + 1) * KC * CH]),
            waits=[("vV", rec["evac_dn"][2 * i + 1])], inc=(xsem, 16))
        if i + 2 < NCH:
            dq_x[i + 2] = qdma(xsem, xs[:, slot * KC * CH:(slot + 1) * KC * CH],
                               d_x[i + 2], waits=[(xsem, dq_out_i)])
    if debug:
        E("scalar", lambda e: e.dma_start(out=d_dbg_wf, in_=wF[:, :]),
          waits=[("vV", rec["wf"][NCH - 1])], inc=("Edbg", 16))

    if debug2:
        for dst, srcb in [(d2_h1, h1r), (d2_h2, h2), (d2_wf, wF), (d2_wb, wbS), (d2_hsb, hsb), (d2_modv, modv), (d2_modt, modt), (d2_silc, siluc), (d2_kt, ktab), (d2_ctv, ctv)]:
            E("scalar", lambda e, d=dst, s=srcb: e.dma_start(out=d, in_=s[:, :]),
              waits=[("vV", cnt["vV"]), ("sS", cnt["sS"]), ("tM", cnt["tM"])], inc=("Edbg", 16))

    for i in range(NCH):
        E("gpsimd", lambda e, i=i: e.dma_start(
            out=d_wfd[0:32, i * CH:(i + 1) * CH], in_=wF[0:32, i * CH:(i + 1) * CH]),
          waits=[("vV", rec["wf"][i])], inc=("Ebnc", 16))
        bounce_done = 16 * (i + 1)
        for half in range(2):
            for e_ in range(NE):
                g = (2 * i + half) * NE + e_
                o = i * CH + half * TC
                w = [("Ebnc", bounce_done)]
                if g >= 8:
                    w.append(("vV", rec["hEv"][g - 8]))
                E("gpsimd", lambda e, g=g, o=o: e.dma_start(
                    out=wbS[:, (g % 8) * TC:(g % 8) * TC + TC],
                    in_=d_wfd[(g % NE):(g % NE) + 1, o:o + TC].broadcast_to((128, TC))),
                  waits=w, inc=("Ewb%d" % (g % 8), 16))

    for v in range(2 * NCH):
        i = v // 2
        for e_ in range(NE):
            g = v * NE + e_
            w = []
            if g >= 2:
                w.append(("tM", rec["gu_free"][g - 2]))
            if g < 2:
                w.append(("tM", t_mod))
            E("sync", lambda e, g=g, e_=e_: e.dma_start(
                out=gup[:, (g % 2) * 2048:(g % 2) * 2048 + 2048],
                in_=d_ew[e_][:, 0:2048]),
              waits=w, inc=("Egu%d" % (g % 2), 16))
            w2 = []
            if g >= 3:
                w2.append(("tM", rec["wslot"][g - 3]))
            E("sync", lambda e, g=g, e_=e_: e.dma_start(
                out=dnb[:, (g % 3) * 1024:(g % 3) * 1024 + 1024],
                in_=d_ew[e_][:, 2048:3072]),
              waits=w2, inc=("Edn%d" % (g % 3), 16))
    for s in sem_names:
        if s[0] == "E" and cnt[s] > 0:
            E("sync", lambda e, s=s: e.wait_ge(sems[s], cnt[s]))

    def runner(eng, qname):
        seen = {}
        for fn, waits, inc in queues[qname]:
            for s, val in waits:
                if val > 0 and seen.get(s, 0) < val:
                    eng.wait_ge(sems[s], val)
                    seen[s] = val
            ins = fn(eng)
            if inc is not None:
                ins.then_inc(sems[inc[0]], inc[1])

    with nc.Block() as block:
        @block.sync
        def _(eng):
            runner(eng, "sync")

        @block.tensor
        def _(eng):
            runner(eng, "tensor")

        @block.vector
        def _(eng):
            runner(eng, "vector")

        @block.gpsimd
        def _(eng):
            runner(eng, "gpsimd")

        @block.scalar
        def _(eng):
            runner(eng, "scalar")

    es.close()
    return nc


def pack_inputs(L0_attn_x, L0_attn_A_log, L0_attn_B, L0_attn_C, L0_attn_D,
                L0_ffn_router_w, L0_ffn_router_bias, L0_ffn_gate_w,
                L0_ffn_up_w, L0_ffn_down_w, L0_ffn_sgate_w, L0_ffn_sup_w,
                L0_ffn_sdown_w, L0_c, L0_wmod):
    f = np.float32
    gw = np.asarray(L0_ffn_gate_w, f)
    uw = np.asarray(L0_ffn_up_w, f)
    dw = np.asarray(L0_ffn_down_w, f)
    ew = np.empty((32, 128, 3072), f)
    ew[:, :, 0:1024] = gw.reshape(32, 128, 8, 128).transpose(0, 3, 2, 1).reshape(32, 128, 1024)
    ew[:, :, 1024:2048] = uw.reshape(32, 128, 8, 128).transpose(0, 3, 2, 1).reshape(32, 128, 1024)
    ew[:, :, 2048:3072] = dw.reshape(32, 8, 128, 128).transpose(0, 3, 1, 2).reshape(32, 128, 1024)
    ew = np.ascontiguousarray(ew)
    sgT = np.ascontiguousarray(
        np.asarray(L0_ffn_sgate_w, f).reshape(4, 128, 8, 128).transpose(3, 2, 0, 1).reshape(128, 4096))
    suT = np.ascontiguousarray(
        np.asarray(L0_ffn_sup_w, f).reshape(4, 128, 8, 128).transpose(3, 2, 0, 1).reshape(128, 4096))
    sdT = np.ascontiguousarray(
        np.asarray(L0_ffn_sdown_w, f).reshape(8, 128, 4, 128).transpose(3, 2, 0, 1).reshape(128, 4096))
    rtT = np.ascontiguousarray(
        np.asarray(L0_ffn_router_w, f).reshape(32, 8, 128).transpose(2, 1, 0).reshape(128, 256))
    rbr = np.ascontiguousarray(np.asarray(L0_ffn_router_bias, f).reshape(1, 32))
    wmods = np.ascontiguousarray(np.asarray(L0_wmod, f).reshape(8, 128, 6144))
    A_all = np.ascontiguousarray(
        np.asarray(L0_attn_A_log, f).reshape(16, 8, 128).transpose(2, 1, 0).reshape(128, 128))
    B_pp = np.ascontiguousarray(
        np.asarray(L0_attn_B, f).reshape(16, 8, 128).transpose(2, 1, 0).reshape(128, 128))
    C_pp = np.ascontiguousarray(
        np.asarray(L0_attn_C, f).reshape(8, 128, 16).transpose(1, 0, 2).reshape(128, 128))
    D_pp = np.ascontiguousarray(np.asarray(L0_attn_D, f).reshape(8, 128).T)
    ident = np.eye(128, dtype=f)
    x = np.asarray(L0_attn_x, f)
    cm = np.asarray(L0_c, f)
    in_maps = []
    for b in range(NCORE):
        xT = np.ascontiguousarray(x[b].T)
        xc = np.ascontiguousarray(
            xT.reshape(8, 128, 4, 512).transpose(2, 1, 0, 3).reshape(4, 128, 4096))
        cT = np.ascontiguousarray(cm[b].reshape(8, 128).T)
        in_maps.append(dict(
            xin=xc, cT=cT, ew=ew, sgT=sgT, suT=suT, sdT=sdT, rtT=rtT, rbr=rbr,
            wmods=wmods, A_all=A_all, B_pp=B_pp, C_pp=C_pp, D_pp=D_pp,
            ident=ident))
    return in_maps


def kernel(**inputs):
    in_maps = pack_inputs(**inputs)
    nc = build_program(debug=False)
    res = run_bass_kernel_spmd(nc, in_maps, core_ids=list(range(NCORE)))
    outs = []
    for b in range(NCORE):
        yo = np.asarray(res.results[b]["yout"])
        xT = yo.reshape(4, 128, 8, 512).transpose(2, 1, 0, 3).reshape(1024, 2048)
        outs.append(xT.T)
    return np.ascontiguousarray(np.stack(outs, axis=0)).astype(np.float32)


# version 18
# speedup vs baseline: 1.0293x; 1.0293x over previous
import contextlib
import numpy as np
import concourse.bass as bass
import concourse.mybir as mybir
from concourse.bass_utils import run_bass_kernel_spmd

F32 = mybir.dt.float32
AF = mybir.ActivationFunctionType
ALU = mybir.AluOpType
AX = mybir.AxisListType.X

NCORE = 8
D = 1024
TOK = 2048
KC = 8
NCH = 4
CH = 512
TC = 256
NE = 32
HE = 128
TAPS = 40
NTAU = 4


def build_program(debug=False, pipe_hooks=True, pipe_fill=True, debug2=False):
    nc = bass.Bass("TRN2", target_bir_lowering=False)

    d_x = nc.dram_tensor("xin", [NCH, 128, KC * CH], F32, kind="ExternalInput").ap()
    d_ew = nc.dram_tensor("ew", [NE, 128, 3072], F32, kind="ExternalInput").ap()
    d_sg = nc.dram_tensor("sgT", [128, 4096], F32, kind="ExternalInput").ap()
    d_su = nc.dram_tensor("suT", [128, 4096], F32, kind="ExternalInput").ap()
    d_sd = nc.dram_tensor("sdT", [128, 4096], F32, kind="ExternalInput").ap()
    d_rt = nc.dram_tensor("rtT", [128, 256], F32, kind="ExternalInput").ap()
    d_rb = nc.dram_tensor("rbr", [1, 32], F32, kind="ExternalInput").ap()
    d_wm = nc.dram_tensor("wmods", [KC, 128, 6144], F32, kind="ExternalInput").ap()
    d_aa = nc.dram_tensor("A_all", [128, 128], F32, kind="ExternalInput").ap()
    d_bb = nc.dram_tensor("B_pp", [128, 128], F32, kind="ExternalInput").ap()
    d_cc = nc.dram_tensor("C_pp", [128, 128], F32, kind="ExternalInput").ap()
    d_dd = nc.dram_tensor("D_pp", [128, 8], F32, kind="ExternalInput").ap()
    d_id = nc.dram_tensor("ident", [128, 128], F32, kind="ExternalInput").ap()
    d_ct = nc.dram_tensor("cT", [128, 8], F32, kind="ExternalInput").ap()
    d_y = nc.dram_tensor("yout", [NCH, 128, KC * CH], F32, kind="ExternalOutput").ap()
    d_wfd = nc.dram_tensor("wfd", [32, TOK], F32, kind="ExternalOutput").ap()
    if debug2:
        d2_h1 = nc.dram_tensor("d2_h1", [128, KC * 576], F32, kind="ExternalOutput").ap()
        d2_h2 = nc.dram_tensor("d2_h2", [128, KC * CH], F32, kind="ExternalOutput").ap()
        d2_wf = nc.dram_tensor("d2_wf", [32, TOK], F32, kind="ExternalOutput").ap()
        d2_wb = nc.dram_tensor("d2_wb", [128, 8 * TC], F32, kind="ExternalOutput").ap()
        d2_hsb = nc.dram_tensor("d2_hsb", [128, 2048], F32, kind="ExternalOutput").ap()
        d2_modv = nc.dram_tensor("d2_modv", [128, 48], F32, kind="ExternalOutput").ap()
        d2_modt = nc.dram_tensor("d2_modt", [128, 48], F32, kind="ExternalOutput").ap()
        d2_silc = nc.dram_tensor("d2_silc", [128, 8], F32, kind="ExternalOutput").ap()
        d2_kt = nc.dram_tensor("d2_kt", [128, TAPS * 8], F32, kind="ExternalOutput").ap()
        d2_ctv = nc.dram_tensor("d2_ctv", [128, 8], F32, kind="ExternalOutput").ap()
    if debug:
        d_dbg_h1 = nc.dram_tensor("dbg_h1", [NCH, 128, KC * 576], F32, kind="ExternalOutput").ap()
        d_dbg_x1 = nc.dram_tensor("dbg_x1", [NCH, 128, KC * CH], F32, kind="ExternalOutput").ap()
        d_dbg_h2 = nc.dram_tensor("dbg_h2", [NCH, 128, KC * CH], F32, kind="ExternalOutput").ap()
        d_dbg_wf = nc.dram_tensor("dbg_wf", [32, TOK], F32, kind="ExternalOutput").ap()
        d_dbg_kt = nc.dram_tensor("dbg_kt", [128, TAPS * 8], F32, kind="ExternalOutput").ap()
        d_dbg_mod = nc.dram_tensor("dbg_mod", [128, 48], F32, kind="ExternalOutput").ap()

    es = contextlib.ExitStack()
    sb = lambda name, shape: es.enter_context(nc.sbuf_tensor(name, shape, F32))

    xs = sb("xs", [128, 2 * KC * CH])
    h1r = sb("h1r", [128, KC * 576])
    h2 = sb("h2", [128, KC * CH])
    gup = sb("gup", [128, 2 * 2048])
    dnb = sb("dnb", [128, 3 * 1024])
    sgT = sb("sgTb", [128, 4096])
    suT = sb("suTb", [128, 4096])
    sdT = sb("sdTb", [128, 4096])
    rtT = sb("rtTb", [128, 256])
    rbr = sb("rbrb", [1, 32])
    aal = sb("aal", [128, 128])
    bpp = sb("bpp", [128, 128])
    cpp = sb("cpp", [128, 128])
    wpos = sb("wpos", [128, 128])
    wneg = sb("wneg", [128, 128])
    pbuf = sb("pbuf", [128, 2 * 128])
    pw = sb("pw", [128, 128])
    dpp = sb("dpp", [128, 8])
    ktab = sb("ktab", [128, TAPS * 8])
    tcolv = sb("tcolv", [128, TAPS])
    ctv = sb("ctv", [128, 8])
    siluc = sb("siluc", [128, 8])
    modv = sb("modv", [128, 48])
    modt = sb("modt", [128, 48])
    idn = sb("idn", [128, 128])
    cst = sb("cst", [128, 4])
    ones1 = es.enter_context(nc.sbuf_tensor("ones1", [1, 128], F32))
    onesc = sb("onesc", [128, 1])
    rows = es.enter_context(nc.sbuf_tensor("rows", [1, 2560], F32))
    R_MEAN, R_Q, R_T, R_R, R_MR = 0, 512, 1024, 1536, 2048
    xsq2 = sb("xsq2", [128, 2 * CH])
    tA = sb("tA", [128, 256])
    tB = sb("tB", [128, 256])
    yacc = sb("yacc", [128, CH])
    sE = sb("sE", [128, 2 * TC])
    tE = sb("tE", [128, TC])
    hEb = sb("hEb", [128, 4 * TC])
    hsb = sb("hsb", [128, 2048])
    ssh = sb("ssh", [128, 2 * CH])
    s0n = sb("s0n", [128, 128])
    s0e = sb("s0e", [128, 128])
    swb = sb("swb", [128, 128])
    mskb = sb("mskb", [128, 128])
    maccb = sb("maccb", [128, 128])
    wTb = sb("wTb", [128, 128])
    r1 = sb("r1", [128, 32])
    wF = es.enter_context(nc.sbuf_tensor("wF", [32, TOK], F32))
    wbS = sb("wbS", [128, 8 * TC])

    pb = [es.enter_context(nc.psum_tensor(f"pb{i}", [128, 512], F32)) for i in range(8)]

    sem_names = ["tM", "vV", "sS", "Eblk", "Ewm0", "Ewm1", "Exs0", "Exs1",
                 "Egu0", "Egu1", "Edn0", "Edn1", "Edn2", "Ebnc",
                 "Ewb0", "Ewb1", "Ewb2", "Ewb3", "Ewb4", "Ewb5", "Ewb6", "Ewb7",
                 "Edbg"]
    sems = {n: es.enter_context(nc.semaphore(n)) for n in sem_names}
    cnt = {n: 0 for n in sem_names}
    queues = {"sync": [], "tensor": [], "vector": [], "scalar": [], "gpsimd": []}

    def E(q, fn, waits=(), inc=None):
        w = tuple((s, v) for (s, v) in waits if v > 0)
        queues[q].append((fn, w, inc))
        if inc is not None:
            cnt[inc[0]] += inc[1]
        return cnt[inc[0]] if inc is not None else None

    spc = sb("spc", [128, 2])

    def SP():
        E("vector", lambda e: e.tensor_copy(spc[:, 0:1], spc[:, 1:2]))

    def EVS(fn, waits=()):
        w = list(waits) + [("vV", cnt["vV"])]
        return E("vector", fn, waits=w, inc=("vV", 1))

    SGC = dict(skip_group_check=True)

    def BARRIER():
        t1 = E("tensor", lambda e, a=cnt["vV"], b=cnt["sS"]: e.matmul(
            pb[7][0:1, 0:1], onesc[:, :], onesc[:, :],
            start=True, stop=True, **SGC),
            waits=[("vV", cnt["vV"]), ("sS", cnt["sS"])], inc=("tM", 1))
        E("vector", lambda e: e.tensor_copy(spc[:, 0:1], spc[:, 1:2]),
          waits=[("tM", t1)], inc=("vV", 1))
        E("scalar", lambda e: e.copy(spc[:, 0:1], spc[:, 1:2]),
          waits=[("tM", t1)], inc=("sS", 1))
        ww = [("vV", cnt["vV"]), ("sS", cnt["sS"])]
        E("tensor", lambda e: e.matmul(pb[7][0:1, 0:1], onesc[:, :], onesc[:, :],
                                       start=True, stop=True, **SGC), waits=ww)

    rec = {
        "res1": [0] * NCH,
        "ln2_apply": [0] * NCH,
        "wf": [0] * NCH,
        "evac_dn": [0] * (2 * NCH),
        "evac_sh": [0] * NCH,
        "wslot": {},
        "gu_free": {},
        "hslot": {},
        "hEv": {},
        "stat_b": [0, 0],
        "pbg": {b: [] for b in range(8)},
    }

    def bank_w(b):
        return list(rec["pbg"][b])

    def bank_r(b, sem, val):
        rec["pbg"][b] = [(sem, val)]

    def dg_wb(i, half, e_):
        g = (2 * i + half) * NE + e_
        return ("Ewb%d" % (g % 8), 16 * (g // 8 + 1))



    def qdma(sem, dst, src, waits=()):
        E("scalar", lambda e, d=dst, s=src: e.dma_start(out=d, in_=s),
          waits=waits, inc=(sem, 16))
        return cnt[sem]
    for dst, src in [(idn[:, :], d_id), (ctv[:, :], d_ct),
                     (aal[:, :], d_aa), (bpp[:, :], d_bb), (cpp[:, :], d_cc),
                     (dpp[:, :], d_dd), (rtT[:, :], d_rt), (rbr[:, :], d_rb),
                     (sgT[:, :], d_sg), (suT[:, :], d_su), (sdT[:, :], d_sd)]:
        qdma("Eblk", dst, src)
    BLK = cnt["Eblk"]
    dq_x = [0] * NCH
    dq_x[0] = qdma("Exs0", xs[:, 0:KC * CH], d_x[0])
    dq_x[1] = qdma("Exs1", xs[:, KC * CH:2 * KC * CH], d_x[1])

    E("vector", lambda e: e.memset(ones1[:, :], 1.0))
    E("vector", lambda e: e.memset(onesc[:, :], 1.0))
    E("vector", lambda e: e.memset(cst[:, 0:1], 1e-5))
    for t in range(TAPS):
        E("vector", lambda e, t=t: e.memset(tcolv[:, t:t + 1], float(t)))
    for k in range(KC):
        E("vector", lambda e, k=k: e.memset(h1r[:, k * 576:k * 576 + 64], 0.0))
    E("vector", lambda e: e.tensor_mul(wpos[:, :], cpp[:, :], bpp[:, :]),
      waits=[("Eblk", BLK)])
    v_w = E("vector", lambda e: e.tensor_scalar_mul(wneg[:, :], wpos[:, :], -1.0),
            inc=("vV", 1))

    s_silu = E("scalar", lambda e: e.activation(siluc[:, :], ctv[:, :], AF.Silu),
               waits=[("Eblk", BLK)], inc=("sS", 1))
    v_pw = {}
    s_p = {}
    for t in range(TAPS):
        slot = (t % 2) * 128
        w_p = [("vV", v_w), ("Eblk", BLK)]
        if t >= 2:
            w_p.append(("vV", v_pw[t - 2]))
        s_p[t] = E("scalar",
                   lambda e, t=t, sl=slot: e.activation(
                       pbuf[:, sl:sl + 128], aal[:, :], AF.Exp,
                       scale=tcolv[:, t:t + 1]),
                   waits=w_p, inc=("sS", 1))
        wsrc = wpos if t % 2 == 0 else wneg
        EVS(lambda e, sl=slot, ws=wsrc: e.tensor_mul(
            pw[:, :], pbuf[:, sl:sl + 128], ws[:, :]),
            waits=[("sS", s_p[t])])
        v_pw[t] = EVS(lambda e, t=t: e.reduce_sum(
            ktab[:, t * 8:(t + 1) * 8],
            pw.rearrange("p (a c) -> p a c", a=8), axis=AX))
    v_kd = E("vector", lambda e: e.tensor_add(ktab[:, 0:8], ktab[:, 0:8], dpp[:, :]),
             waits=[("vV", v_pw[0])], inc=("vV", 1))

    t_piece = [0, 0]
    t_slab = 0
    for k in range(KC):
        for p in range(3):
            n = k * 3 + p
            slot = (n % 2) * 2048
            w = [("tM", t_piece[n % 2])] if t_piece[n % 2] else []
            wmsem = "Ewm%d" % (n % 2)
            dq_slab = qdma(wmsem, gup[:, slot:slot + 2048],
                           d_wm[k][:, p * 2048:(p + 1) * 2048], waits=w)
            for jj in range(16):
                j = p * 16 + jj
                wj = [(wmsem, dq_slab), ("sS", s_silu)] if jj == 0 else []
                inc = ("tM", 1) if jj == 15 else None
                t_slab = E("tensor",
                           lambda e, k=k, j=j, jj=jj, sl=slot: e.matmul(
                               pb[4][:, j:j + 1], gup[:, sl + jj * 128: sl + (jj + 1) * 128],
                               siluc[:, k:k + 1], start=(k == 0 and j == 0),
                               stop=(k == KC - 1), **SGC),
                           waits=wj, inc=inc) or t_slab
            t_piece[n % 2] = t_slab
    t_mod = t_slab
    v_mev = EVS(lambda e: e.tensor_copy(modv[:, :], pb[4][:, 0:48]),
                waits=[("tM", t_mod)])
    bank_r(4, "vV", v_mev)
    EVS(lambda e: e.tensor_copy(modt[:, 0:8], modv[:, 0:8]))
    EVS(lambda e: e.tensor_scalar_add(modt[:, 8:16], modv[:, 8:16], 1.0))
    EVS(lambda e: e.tensor_copy(modt[:, 24:32], modv[:, 24:32]))
    EVS(lambda e: e.tensor_scalar_add(modt[:, 32:40], modv[:, 32:40], 1.0))
    v_mod = 0
    for (src0, dst0) in [(16, 16), (40, 40)]:
        EVS(lambda e, s=src0: e.tensor_scalar_mul(
            tA[:, 0:8], modv[:, s:s + 8], -1.0))
        EVS(lambda e, s=src0: e.tensor_scalar(
            out=tB[:, 0:8], in0=modv[:, s:s + 8], scalar1=0.0, scalar2=None,
            op0=ALU.is_ge))
        EVS(lambda e, s=src0: e.tensor_max(
            tA[:, 0:8], tA[:, 0:8], modv[:, s:s + 8]))
        EVS(lambda e: e.tensor_scalar(
            out=tB[:, 0:8], in0=tB[:, 0:8], scalar1=2.0, scalar2=-1.0,
            op0=ALU.mult, op1=ALU.add))
        EVS(lambda e: e.tensor_scalar_max(tA[:, 0:8], tA[:, 0:8], 0.25))
        v_mod = EVS(lambda e, d=dst0: e.tensor_mul(
            modt[:, d:d + 8], tB[:, 0:8], tA[:, 0:8]))


    if debug:
        E("scalar", lambda e: e.dma_start(out=d_dbg_kt, in_=ktab[:, :]),
          waits=[("vV", v_kd)], inc=("Edbg", 16))
        E("scalar", lambda e: e.dma_start(out=d_dbg_mod, in_=modt[:, :]),
          waits=[("vV", v_mod)], inc=("Edbg", 16))

    def ln_stats(i, slot, src_guard_v):
        base = slot * KC * CH
        t_b = 0
        for k in range(KC):
            xsl = xs[:, base + k * CH: base + (k + 1) * CH]
            sq = xsq2[:, (k % 2) * CH:(k % 2 + 1) * CH]
            v_sq = E("vector", lambda e, x=xsl, q=sq: e.tensor_mul(q, x, x),
                     waits=[("tM", rec["stat_b"][k % 2])], inc=("vV", 1))
            wa = ([("vV", src_guard_v), ("Exs%d" % (i % 2), dq_x[i])] + bank_w(3)) if k == 0 else []
            E("tensor", lambda e, x=xsl, k=k: e.matmul(
                pb[3][0:1, 0:512], onesc[:, :], x,
                start=(k == 0), stop=(k == KC - 1), **SGC), waits=wa)
            t_b = E("tensor", lambda e, q=sq, k=k: e.matmul(
                pb[3][32:33, 0:512], onesc[:, :], q,
                start=False, stop=(k == KC - 1), **SGC),
                waits=[("vV", v_sq)], inc=("tM", 1))
            rec["stat_b"][k % 2] = t_b
        EVS(lambda e: e.tensor_scalar_mul(
            rows[0:1, R_MEAN:R_MEAN + 512], pb[3][0:1, 0:512], 1.0 / 1024.0),
            waits=[("tM", t_b)])
        EVS(lambda e: e.tensor_scalar_mul(
            rows[0:1, R_Q:R_Q + 512], pb[3][32:33, 0:512], 1.0 / 1024.0))
        EVS(lambda e: e.tensor_mul(
            rows[0:1, R_T:R_T + 512], rows[0:1, R_MEAN:R_MEAN + 512],
            rows[0:1, R_MEAN:R_MEAN + 512]))
        v_var = EVS(lambda e: e.tensor_sub(
            rows[0:1, R_Q:R_Q + 512], rows[0:1, R_Q:R_Q + 512],
            rows[0:1, R_T:R_T + 512]))
        s_sd = E("scalar", lambda e: e.activation(
            rows[0:1, R_T:R_T + 512], rows[0:1, R_Q:R_Q + 512], AF.Sqrt,
            bias=cst[0:1, 0:1]), waits=[("vV", v_var)], inc=("sS", 1))
        EVS(lambda e: e.reciprocal(
            rows[0:1, R_R:R_R + 512], rows[0:1, R_T:R_T + 512]),
            waits=[("sS", s_sd)])
        v_rows = EVS(lambda e: e.tensor_mul(
            rows[0:1, R_MR:R_MR + 512], rows[0:1, R_MEAN:R_MEAN + 512],
            rows[0:1, R_R:R_R + 512]))
        return v_rows

    def ln_bcast_apply(slot, half, ca_col, sa_col, dst, dst_off, v_guard):
        base = slot * KC * CH
        off = half * 256
        E("tensor", lambda e, o=off: e.matmul(
            pb[3][:, 0:256], ones1[:, :], rows[0:1, R_R + o:R_R + o + 256],
            start=True, stop=True, **SGC),
          waits=[("vV", v_guard)])
        t_bc = E("tensor", lambda e, o=off: e.matmul(
            pb[3][:, 256:512], ones1[:, :], rows[0:1, R_MR + o:R_MR + o + 256],
            start=False, stop=True, **SGC), inc=("tM", 1))
        v_apply = v_guard
        for k in range(KC):
            xsl = xs[:, base + k * CH + off: base + k * CH + off + 256]
            E("vector", lambda e, x=xsl: e.tensor_mul(tA[:, :], x, pb[3][:, 0:256]),
              waits=[("tM", t_bc)] if k == 0 else [])
            E("vector", lambda e: e.tensor_sub(tA[:, :], tA[:, :], pb[3][:, 256:512]))
            dsl = dst[:, dst_off(k, half): dst_off(k, half) + 256]
            inc = ("vV", 1) if k == KC - 1 else None
            v_apply = E("vector", lambda e, d=dsl, k=k: e.tensor_scalar(
                out=d, in0=tA[:, :], scalar1=modt[:, ca_col + k:ca_col + k + 1],
                scalar2=modt[:, sa_col + k:sa_col + k + 1],
                op0=ALU.mult, op1=ALU.add), inc=inc) or v_apply
        bank_r(3, "vV", v_apply)
        return v_apply

    def ln_phase(i, slot, ca_col, sa_col, dst, dst_off, src_guard_v):
        v_rows = ln_stats(i, slot, src_guard_v)
        v0 = ln_bcast_apply(slot, 0, ca_col, sa_col, dst, dst_off, v_rows)
        return ln_bcast_apply(slot, 1, ca_col, sa_col, dst, dst_off, v0)

    def conv_ops_list(i, slot):
        base = slot * KC * CH
        ops = []
        for k in range(KC):
            hb = k * 576
            ops.append(lambda hb=hb, k=k: E("vector", lambda e: e.tensor_scalar_mul(
                yacc[:, :], h1r[:, hb + 64:hb + 64 + CH], ktab[:, k:k + 1])))
            for t in range(1, TAPS):
                ops.append(lambda hb=hb, t=t, k=k: E("vector", lambda e: e.scalar_tensor_tensor(
                    out=yacc[:, :], in0=h1r[:, hb + 64 - t:hb + 64 - t + CH],
                    scalar=ktab[:, t * 8 + k:t * 8 + k + 1], in1=yacc[:, :],
                    op0=ALU.mult, op1=ALU.add)))
            xsl = xs[:, base + k * CH: base + (k + 1) * CH]
            inc = ("vV", 1) if k == KC - 1 else None
            ops.append(lambda x=xsl, k=k, inc=inc, i=i: rec["res1"].__setitem__(
                i, E("vector", lambda e: e.scalar_tensor_tensor(
                    out=x, in0=yacc[:, :], scalar=modt[:, 16 + k:17 + k], in1=x,
                    op0=ALU.mult, op1=ALU.add), inc=inc) or rec["res1"][i]))
        return ops

    def conv_phase(i, slot):
        for op in conv_ops_list(i, slot):
            op()
        return rec["res1"][i]

    def halo_phase():
        for k in range(KC):
            hb = k * 576
            E("vector", lambda e, hb=hb: e.tensor_copy(
                h1r[:, hb:hb + 64], h1r[:, hb + 512:hb + 576]))

    def router_phase(i):
        t_l = {}
        v_mx = {}
        s_exp = {}
        v_wt = {}
        t_tr = {}
        v_wf = 0
        def emit_psl(tau):
            bk = pb[2 + tau % 2]
            for k in range(KC):
                w = []
                if k == 0:
                    w = [("vV", rec["ln2_apply"][i]), ("Eblk", BLK)]
                    if tau < 2:
                        w += bank_w(2 + tau % 2)
                    else:
                        w += [("sS", s_exp[tau - 2]), ("vV", v_mx[tau - 2])]
                E("tensor", lambda e, k=k, tau=tau, bk=bk: e.matmul(
                    bk[:, 0:32],
                    h2[:, k * CH + tau * 128: k * CH + tau * 128 + 128],
                    rtT[:, k * 32:(k + 1) * 32],
                    start=(k == 0), stop=False, **SGC), waits=w)
            t_l[tau] = E("tensor", lambda e, bk=bk: e.matmul(
                bk[:, 0:32], ones1[:, :], rbr[0:1, :],
                start=False, stop=True, **SGC), inc=("tM", 1))
            E("vector", lambda e, tau=tau, bk=bk: e.reduce_max(
                r1[:, tau * 8:tau * 8 + 1], bk[:, 0:32], axis=AX),
              waits=[("tM", t_l[tau])])

        def emit_mxn_exp(taus):
            for tau in taus:
                v_mx[tau] = E("vector", lambda e, tau=tau: e.tensor_scalar_mul(
                    r1[:, tau * 8 + 1:tau * 8 + 2], r1[:, tau * 8:tau * 8 + 1], -1.0),
                    inc=("vV", 1))
            for tau in taus:
                bk = pb[2 + tau % 2]
                s_exp[tau] = E("scalar", lambda e, tau=tau, bk=bk: e.activation(
                    s0e[:, tau * 32:(tau + 1) * 32], bk[:, 0:32], AF.Exp,
                    bias=r1[:, tau * 8 + 1:tau * 8 + 2]),
                    waits=[("vV", v_mx[tau])], inc=("sS", 1))

        emit_psl(0); emit_psl(1)
        emit_mxn_exp([0, 1])
        emit_psl(2); emit_psl(3)
        emit_mxn_exp([2, 3])
        def RR(fn_of_tau, waits_of_tau=None):
            out = {}
            for tau in range(NTAU):
                w = waits_of_tau(tau) if waits_of_tau else []
                out[tau] = E("vector", fn_of_tau(tau), waits=w, inc=("vV", 1))
            return out

        RR(lambda tau: (lambda e, tau=tau: e.reduce_sum(
            r1[:, tau * 8 + 2:tau * 8 + 3], s0e[:, tau * 32:(tau + 1) * 32], axis=AX)),
           lambda tau: [("sS", s_exp[tau])])
        RR(lambda tau: (lambda e, tau=tau: e.reciprocal(
            r1[:, tau * 8 + 3:tau * 8 + 4], r1[:, tau * 8 + 2:tau * 8 + 3])))
        RR(lambda tau: (lambda e, tau=tau: e.tensor_scalar_mul(
            s0n[:, tau * 32:(tau + 1) * 32], s0e[:, tau * 32:(tau + 1) * 32],
            r1[:, tau * 8 + 3:tau * 8 + 4])))
        RR(lambda tau: (lambda e, tau=tau: e.tensor_copy(
            swb[:, tau * 32:(tau + 1) * 32], s0n[:, tau * 32:(tau + 1) * 32])))
        for k4 in range(4):
            RR(lambda tau: (lambda e, tau=tau: e.reduce_max(
                r1[:, tau * 8 + 4:tau * 8 + 5], swb[:, tau * 32:(tau + 1) * 32], axis=AX)))
            if k4 == 0:
                RR(lambda tau: (lambda e, tau=tau: e.tensor_copy(
                    r1[:, tau * 8 + 5:tau * 8 + 6], r1[:, tau * 8 + 4:tau * 8 + 5])))
            else:
                RR(lambda tau: (lambda e, tau=tau: e.tensor_add(
                    r1[:, tau * 8 + 5:tau * 8 + 6], r1[:, tau * 8 + 5:tau * 8 + 6],
                    r1[:, tau * 8 + 4:tau * 8 + 5])))
            RR(lambda tau: (lambda e, tau=tau: e.tensor_scalar(
                out=mskb[:, tau * 32:(tau + 1) * 32],
                in0=swb[:, tau * 32:(tau + 1) * 32],
                scalar1=r1[:, tau * 8 + 4:tau * 8 + 5], scalar2=None,
                op0=ALU.is_ge)))
            if k4 == 0:
                RR(lambda tau: (lambda e, tau=tau: e.tensor_copy(
                    maccb[:, tau * 32:(tau + 1) * 32], mskb[:, tau * 32:(tau + 1) * 32])))
            else:
                RR(lambda tau: (lambda e, tau=tau: e.tensor_add(
                    maccb[:, tau * 32:(tau + 1) * 32], maccb[:, tau * 32:(tau + 1) * 32],
                    mskb[:, tau * 32:(tau + 1) * 32])))
            if k4 < 3:
                RR(lambda tau: (lambda e, tau=tau: e.scalar_tensor_tensor(
                    out=swb[:, tau * 32:(tau + 1) * 32],
                    in0=mskb[:, tau * 32:(tau + 1) * 32], scalar=-4.0,
                    in1=swb[:, tau * 32:(tau + 1) * 32],
                    op0=ALU.mult, op1=ALU.add)))
        RR(lambda tau: (lambda e, tau=tau: e.tensor_scalar(
            out=r1[:, tau * 8 + 6:tau * 8 + 7], in0=r1[:, tau * 8 + 5:tau * 8 + 6],
            scalar1=1e-12, scalar2=None, op0=ALU.max)))
        RR(lambda tau: (lambda e, tau=tau: e.reciprocal(
            r1[:, tau * 8 + 7:tau * 8 + 8], r1[:, tau * 8 + 6:tau * 8 + 7])))
        RR(lambda tau: (lambda e, tau=tau: e.tensor_scalar_mul(
            s0n[:, tau * 32:(tau + 1) * 32], s0n[:, tau * 32:(tau + 1) * 32],
            r1[:, tau * 8 + 7:tau * 8 + 8])))
        v_wt = RR(lambda tau: (lambda e, tau=tau: e.tensor_mul(
            wTb[:, tau * 32:(tau + 1) * 32], s0n[:, tau * 32:(tau + 1) * 32],
            maccb[:, tau * 32:(tau + 1) * 32])))
        for tau in range(NTAU):
            tb = pb[4 + tau % 2]
            w = [("vV", v_wt[tau])] + (bank_w(4 + tau % 2) if tau < 2 else
                                       [("vV", t_tr[tau - 2])])
            t_ = E("tensor", lambda e, tau=tau, tb=tb: e.matmul(
                tb[0:32, 0:128], wTb[:, tau * 32:(tau + 1) * 32], idn[:, :],
                is_transpose=True, start=True, stop=True, **SGC),
                waits=w, inc=("tM", 1))
            g = i * NTAU + tau
            v_wf = E("vector", lambda e, g=g, tb=tb: e.tensor_copy(
                wF[0:32, g * 128:(g + 1) * 128], tb[0:32, 0:128]),
                waits=[("tM", t_)], inc=("vV", 1))
            t_tr[tau] = v_wf
        rec["pbg"][2] = [("vV", v_mx[2]), ("sS", s_exp[2])]
        rec["pbg"][3] = [("vV", v_mx[3]), ("sS", s_exp[3])]
        bank_r(4, "vV", t_tr[2])
        bank_r(5, "vV", t_tr[3])
        rec["wf"][i] = v_wf
        return v_wf

    def experts_phase(i, hooks=None, filler=None):
        base = i * KC * CH
        fpop = [0]
        if filler:
            per = max(1, (len(filler) + 51) // 52)
        for half in range(2):
            tcv = 2 * i + half
            off = half * TC
            t_gate = {}
            t_up = {}
            v_he = {}

            def emit_gup(e_):
                g = tcv * NE + e_
                slot = (g % 2) * 2048
                ub = pb[1 + e_ % 2]
                for k in range(KC):
                    w = []
                    if k == 0:
                        w = [("Egu%d" % (g % 2), 16 * (g // 2 + 1))] + bank_w(0)
                    inc = ("tM", 1) if k == KC - 1 else None
                    t_gate[e_] = E("tensor", lambda e, k=k, sl=slot, o=off: e.matmul(
                        pb[0][:, 0:TC], gup[:, sl + k * 128: sl + (k + 1) * 128],
                        h2[:, k * CH + o: k * CH + o + TC],
                        start=(k == 0), stop=(k == KC - 1), **SGC),
                        waits=w, inc=inc) or t_gate.get(e_)
                for k in range(KC):
                    w = bank_w(1 + e_ % 2) if k == 0 else []
                    inc = ("tM", 1) if k == KC - 1 else None
                    t_up[e_] = E("tensor", lambda e, k=k, sl=slot, o=off, ub=ub: e.matmul(
                        ub[:, 0:TC], gup[:, sl + 1024 + k * 128: sl + 1024 + (k + 1) * 128],
                        h2[:, k * CH + o: k * CH + o + TC],
                        start=(k == 0), stop=(k == KC - 1), **SGC),
                        waits=w, inc=inc) or t_up.get(e_)
                rec["gu_free"][g] = t_up[e_]
                s_si = E("scalar", lambda e, e_=e_: e.activation(
                    sE[:, (e_ % 2) * TC:(e_ % 2 + 1) * TC], pb[0][:, 0:TC], AF.Silu),
                    waits=[("tM", t_gate[e_]), ("vV", rec.get("sE%d" % (e_ % 2), 0))],
                    inc=("sS", 1))
                bank_r(0, "sS", s_si)
                v_t = E("vector", lambda e, e_=e_, ub=ub: e.tensor_mul(
                    tE[:, :], ub[:, 0:TC], sE[:, (e_ % 2) * TC:(e_ % 2 + 1) * TC]),
                    waits=[("tM", t_up[e_]), ("sS", s_si)], inc=("vV", 1))
                bank_r(1 + e_ % 2, "vV", v_t)
                rec["sE%d" % (e_ % 2)] = v_t
                hs = (g % 4) * TC
                wh = [("tM", rec["hslot"].get(g - 4, 0)), dg_wb(i, half, e_)]
                v_he[e_] = E("vector", lambda e, hs=hs, g=g: e.tensor_mul(
                    hEb[:, hs:hs + TC], tE[:, :], wbS[:, (g % 8) * TC:(g % 8) * TC + TC]),
                    waits=wh, inc=("vV", 1))
                rec["hEv"][g] = v_he[e_]

            def emit_down(e_):
                g = tcv * NE + e_
                slot3 = (g % 3) * 1024
                hs = (g % 4) * TC
                t_d = 0
                for c in range(KC):
                    bank = pb[4 + c // 2]
                    po = (c % 2) * 256
                    w = []
                    if c == 0:
                        w = [("vV", v_he[e_]), ("Edn%d" % (g % 3), 16 * (g // 3 + 1))]
                        if e_ == 0:
                            for bnk in range(4, 8):
                                w += bank_w(bnk)
                    start = (e_ == 0 and c % 2 == 0)
                    inc = ("tM", 1) if c == KC - 1 else None
                    t_d = E("tensor", lambda e, c=c, sl=slot3, b=bank, po=po, s=start: e.matmul(
                        b[:, po:po + TC], dnb[:, sl + c * 128: sl + (c + 1) * 128],
                        hEb[:, hs:hs + TC],
                        start=s, stop=(e_ == NE - 1), **SGC),
                        waits=w, inc=inc) or t_d
                rec["wslot"][g] = t_d
                rec["hslot"][g] = t_d

            for e_ in range(NE):
                emit_gup(e_)
                if e_ >= 2:
                    emit_down(e_ - 2)
                if hooks and (half, e_) in hooks:
                    hooks[(half, e_)]()
                if filler and (half, e_) >= (0, 12):
                    n = 0
                    while fpop[0] < len(filler) and n < per:
                        filler[fpop[0]]()
                        fpop[0] += 1
                        n += 1
            emit_down(NE - 2)
            emit_down(NE - 1)
            t_last = rec["wslot"][tcv * NE + NE - 1]
            v_ev = 0
            for c in range(KC):
                bank = pb[4 + c // 2]
                po = (c % 2) * 256
                xsl = xs[:, (i % 2) * KC * CH + c * CH + off: (i % 2) * KC * CH + c * CH + off + TC]
                w = [("tM", t_last)] if c == 0 else []
                inc = ("vV", 1) if c == KC - 1 else None
                v_ev = E("vector", lambda e, x=xsl, b=bank, po=po, c=c: e.scalar_tensor_tensor(
                    out=x, in0=b[:, po:po + TC], scalar=modt[:, 40 + c:41 + c], in1=x,
                    op0=ALU.mult, op1=ALU.add), waits=w, inc=inc) or v_ev
            rec["evac_dn"][tcv] = v_ev
            for bnk in range(4, 8):
                bank_r(bnk, "vV", v_ev)
        if filler:
            while fpop[0] < len(filler):
                filler[fpop[0]]()
                fpop[0] += 1

    def shared_phase(i):
        base = (i % 2) * KC * CH
        v_hs = 0
        s_sh = {}
        for j in range(4):
            for k in range(KC):
                w = []
                if k == 0:
                    w = [("Eblk", BLK)] + bank_w(0)
                inc = ("tM", 1) if k == KC - 1 else None
                t_sg = E("tensor", lambda e, j=j, k=k: e.matmul(
                    pb[0][:, 0:512], sgT[:, (k * 4 + j) * 128:(k * 4 + j + 1) * 128],
                    h2[:, k * CH:(k + 1) * CH],
                    start=(k == 0), stop=(k == KC - 1), **SGC), waits=w, inc=inc)
                if inc:
                    t_sg_last = t_sg
            s_sh[j] = E("scalar", lambda e, j=j: e.activation(
                ssh[:, (j % 2) * CH:(j % 2 + 1) * CH], pb[0][:, 0:512], AF.Silu),
                waits=[("tM", t_sg_last), ("vV", rec.get("ssh%d" % (j % 2), 0))],
                inc=("sS", 1))
            bank_r(0, "sS", s_sh[j])
            for k in range(KC):
                w = []
                if k == 0:
                    w = [("Eblk", BLK)] + bank_w(1)
                inc = ("tM", 1) if k == KC - 1 else None
                t_su = E("tensor", lambda e, j=j, k=k: e.matmul(
                    pb[1][:, 0:512], suT[:, (k * 4 + j) * 128:(k * 4 + j + 1) * 128],
                    h2[:, k * CH:(k + 1) * CH],
                    start=(k == 0), stop=(k == KC - 1), **SGC), waits=w, inc=inc)
                if inc:
                    t_su_last = t_su
            v_hs = E("vector", lambda e, j=j: e.tensor_mul(
                hsb[:, j * 512:(j + 1) * 512], pb[1][:, 0:512],
                ssh[:, (j % 2) * CH:(j % 2 + 1) * CH]),
                waits=[("tM", t_su_last), ("sS", s_sh[j])], inc=("vV", 1))
            bank_r(1, "vV", v_hs)
            rec["ssh%d" % (j % 2)] = v_hs
        v_ev = 0
        for c in range(KC):
            bank = pb[2 + (c % 2)]
            t_sd = 0
            for j in range(4):
                w = []
                if j == 0:
                    w = [("Eblk", BLK), ("vV", v_hs)] + bank_w(2 + c % 2)
                inc = ("tM", 1) if j == 3 else None
                t_sd = E("tensor", lambda e, j=j, c=c, b=bank: e.matmul(
                    b[:, 0:512], sdT[:, (j * 8 + c) * 128:(j * 8 + c + 1) * 128],
                    hsb[:, j * 512:(j + 1) * 512],
                    start=(j == 0), stop=(j == 3), **SGC), waits=w, inc=inc) or t_sd
            xsl = xs[:, base + c * CH:(base + (c + 1) * CH)]
            inc = ("vV", 1)
            v_ev = E("vector", lambda e, x=xsl, b=bank, c=c: e.scalar_tensor_tensor(
                out=x, in0=b[:, 0:512], scalar=modt[:, 40 + c:41 + c], in1=x,
                op0=ALU.mult, op1=ALU.add), waits=[("tM", t_sd)], inc=inc)
            bank_r(2 + c % 2, "vV", v_ev)
        rec["evac_sh"][i] = v_ev

    def h1off(k, half):
        return k * 576 + 64 + half * 256

    def h2off(k, half):
        return k * CH + half * 256

    for i in range(NCH):
        slot = i % 2
        if i == 0:
            v_h1 = ln_phase(0, 0, 8, 0, h1r, h1off, src_guard_v=v_mod)
            if debug:
                E("scalar", lambda e: e.dma_start(out=d_dbg_h1[0], in_=h1r[:, :]),
                  waits=[("vV", v_h1)], inc=("Edbg", 16))
            conv_phase(0, 0)
        if debug:
            E("scalar", lambda e, i=i, s=slot: e.dma_start(
                out=d_dbg_x1[i], in_=xs[:, s * KC * CH:(s + 1) * KC * CH]),
              waits=[("vV", rec["res1"][i])], inc=("Edbg", 16))
        v_h2 = ln_phase(i, slot, ca_col=32, sa_col=24, dst=h2,
                        dst_off=h2off, src_guard_v=rec["res1"][i])
        rec["ln2_apply"][i] = v_h2
        if debug:
            E("scalar", lambda e, i=i: e.dma_start(out=d_dbg_h2[i], in_=h2[:, :]),
              waits=[("vV", v_h2)], inc=("Edbg", 16))
        router_phase(i)
        shared_phase(i)
        hooks = None
        fill = None
        if i + 1 < NCH and (pipe_hooks or pipe_fill):
            ns = (i + 1) % 2
            fill = conv_ops_list(i + 1, ns)

            def hook_stats(i=i, ns=ns):
                halo_phase()
                v = ln_stats(i + 1, ns, 0)
                rec["_v_rows"] = v

            def hook_b0(i=i, ns=ns):
                rec["_v_ap0"] = ln_bcast_apply(ns, 0, 8, 0, h1r, h1off,
                                               rec["_v_rows"])

            def hook_b1(i=i, ns=ns):
                ln_bcast_apply(ns, 1, 8, 0, h1r, h1off, rec["_v_ap0"])

            hooks = {(0, 2): hook_stats, (0, 6): hook_b0, (0, 10): hook_b1}
            if not pipe_fill:
                fill = None
        experts_phase(i, hooks=hooks, filler=fill)
        if i + 1 < NCH and hooks is not None and fill is None and pipe_hooks:
            conv_phase(i + 1, (i + 1) % 2)
        if i + 1 < NCH and hooks is None:
            halo_phase()
            vr = ln_stats(i + 1, (i + 1) % 2, 0)
            v0 = ln_bcast_apply((i + 1) % 2, 0, 8, 0, h1r, h1off, vr)
            ln_bcast_apply((i + 1) % 2, 1, 8, 0, h1r, h1off, v0)
            conv_phase(i + 1, (i + 1) % 2)
        xsem = "Exs%d" % slot
        dq_out_i = E("scalar", lambda e, i=i, s=slot: e.dma_start(
            out=d_y[i], in_=xs[:, s * KC * CH:(s + 1) * KC * CH]),
            waits=[("vV", rec["evac_dn"][2 * i + 1])], inc=(xsem, 16))
        if i + 2 < NCH:
            dq_x[i + 2] = qdma(xsem, xs[:, slot * KC * CH:(slot + 1) * KC * CH],
                               d_x[i + 2], waits=[(xsem, dq_out_i)])
    if debug:
        E("scalar", lambda e: e.dma_start(out=d_dbg_wf, in_=wF[:, :]),
          waits=[("vV", rec["wf"][NCH - 1])], inc=("Edbg", 16))

    if debug2:
        for dst, srcb in [(d2_h1, h1r), (d2_h2, h2), (d2_wf, wF), (d2_wb, wbS), (d2_hsb, hsb), (d2_modv, modv), (d2_modt, modt), (d2_silc, siluc), (d2_kt, ktab), (d2_ctv, ctv)]:
            E("scalar", lambda e, d=dst, s=srcb: e.dma_start(out=d, in_=s[:, :]),
              waits=[("vV", cnt["vV"]), ("sS", cnt["sS"]), ("tM", cnt["tM"])], inc=("Edbg", 16))

    for i in range(NCH):
        E("gpsimd", lambda e, i=i: e.dma_start(
            out=d_wfd[0:32, i * CH:(i + 1) * CH], in_=wF[0:32, i * CH:(i + 1) * CH]),
          waits=[("vV", rec["wf"][i])], inc=("Ebnc", 16))
        bounce_done = 16 * (i + 1)
        for half in range(2):
            for e_ in range(NE):
                g = (2 * i + half) * NE + e_
                o = i * CH + half * TC
                w = [("Ebnc", bounce_done)]
                if g >= 8:
                    w.append(("vV", rec["hEv"][g - 8]))
                E("gpsimd", lambda e, g=g, o=o: e.dma_start(
                    out=wbS[:, (g % 8) * TC:(g % 8) * TC + TC],
                    in_=d_wfd[(g % NE):(g % NE) + 1, o:o + TC].broadcast_to((128, TC))),
                  waits=w, inc=("Ewb%d" % (g % 8), 16))

    for v in range(2 * NCH):
        i = v // 2
        for e_ in range(NE):
            g = v * NE + e_
            w = []
            if g >= 2:
                w.append(("tM", rec["gu_free"][g - 2]))
            if g < 2:
                w.append(("tM", t_mod))
            E("sync", lambda e, g=g, e_=e_: e.dma_start(
                out=gup[:, (g % 2) * 2048:(g % 2) * 2048 + 2048],
                in_=d_ew[e_][:, 0:2048]),
              waits=w, inc=("Egu%d" % (g % 2), 16))
            w2 = []
            if g >= 3:
                w2.append(("tM", rec["wslot"][g - 3]))
            E("sync", lambda e, g=g, e_=e_: e.dma_start(
                out=dnb[:, (g % 3) * 1024:(g % 3) * 1024 + 1024],
                in_=d_ew[e_][:, 2048:3072]),
              waits=w2, inc=("Edn%d" % (g % 3), 16))
    for s in sem_names:
        if s[0] == "E" and cnt[s] > 0:
            E("sync", lambda e, s=s: e.wait_ge(sems[s], cnt[s]))

    def runner(eng, qname):
        seen = {}
        for fn, waits, inc in queues[qname]:
            for s, val in waits:
                if val > 0 and seen.get(s, 0) < val:
                    eng.wait_ge(sems[s], val)
                    seen[s] = val
            ins = fn(eng)
            if inc is not None:
                ins.then_inc(sems[inc[0]], inc[1])

    with nc.Block() as block:
        @block.sync
        def _(eng):
            runner(eng, "sync")

        @block.tensor
        def _(eng):
            runner(eng, "tensor")

        @block.vector
        def _(eng):
            runner(eng, "vector")

        @block.gpsimd
        def _(eng):
            runner(eng, "gpsimd")

        @block.scalar
        def _(eng):
            runner(eng, "scalar")

    es.close()
    return nc


def pack_inputs(L0_attn_x, L0_attn_A_log, L0_attn_B, L0_attn_C, L0_attn_D,
                L0_ffn_router_w, L0_ffn_router_bias, L0_ffn_gate_w,
                L0_ffn_up_w, L0_ffn_down_w, L0_ffn_sgate_w, L0_ffn_sup_w,
                L0_ffn_sdown_w, L0_c, L0_wmod):
    f = np.float32
    gw = np.asarray(L0_ffn_gate_w, f)
    uw = np.asarray(L0_ffn_up_w, f)
    dw = np.asarray(L0_ffn_down_w, f)
    ew = np.empty((32, 128, 3072), f)
    ew[:, :, 0:1024] = gw.reshape(32, 128, 8, 128).transpose(0, 3, 2, 1).reshape(32, 128, 1024)
    ew[:, :, 1024:2048] = uw.reshape(32, 128, 8, 128).transpose(0, 3, 2, 1).reshape(32, 128, 1024)
    ew[:, :, 2048:3072] = dw.reshape(32, 8, 128, 128).transpose(0, 3, 1, 2).reshape(32, 128, 1024)
    ew = np.ascontiguousarray(ew)
    sgT = np.ascontiguousarray(
        np.asarray(L0_ffn_sgate_w, f).reshape(4, 128, 8, 128).transpose(3, 2, 0, 1).reshape(128, 4096))
    suT = np.ascontiguousarray(
        np.asarray(L0_ffn_sup_w, f).reshape(4, 128, 8, 128).transpose(3, 2, 0, 1).reshape(128, 4096))
    sdT = np.ascontiguousarray(
        np.asarray(L0_ffn_sdown_w, f).reshape(8, 128, 4, 128).transpose(3, 2, 0, 1).reshape(128, 4096))
    rtT = np.ascontiguousarray(
        np.asarray(L0_ffn_router_w, f).reshape(32, 8, 128).transpose(2, 1, 0).reshape(128, 256))
    rbr = np.ascontiguousarray(np.asarray(L0_ffn_router_bias, f).reshape(1, 32))
    wmods = np.ascontiguousarray(np.asarray(L0_wmod, f).reshape(8, 128, 6144))
    A_all = np.ascontiguousarray(
        np.asarray(L0_attn_A_log, f).reshape(16, 8, 128).transpose(2, 1, 0).reshape(128, 128))
    B_pp = np.ascontiguousarray(
        np.asarray(L0_attn_B, f).reshape(16, 8, 128).transpose(2, 1, 0).reshape(128, 128))
    C_pp = np.ascontiguousarray(
        np.asarray(L0_attn_C, f).reshape(8, 128, 16).transpose(1, 0, 2).reshape(128, 128))
    D_pp = np.ascontiguousarray(np.asarray(L0_attn_D, f).reshape(8, 128).T)
    ident = np.eye(128, dtype=f)
    x = np.asarray(L0_attn_x, f)
    cm = np.asarray(L0_c, f)
    in_maps = []
    for b in range(NCORE):
        xT = np.ascontiguousarray(x[b].T)
        xc = np.ascontiguousarray(
            xT.reshape(8, 128, 4, 512).transpose(2, 1, 0, 3).reshape(4, 128, 4096))
        cT = np.ascontiguousarray(cm[b].reshape(8, 128).T)
        in_maps.append(dict(
            xin=xc, cT=cT, ew=ew, sgT=sgT, suT=suT, sdT=sdT, rtT=rtT, rbr=rbr,
            wmods=wmods, A_all=A_all, B_pp=B_pp, C_pp=C_pp, D_pp=D_pp,
            ident=ident))
    return in_maps


def kernel(**inputs):
    in_maps = pack_inputs(**inputs)
    nc = build_program(debug=False)
    res = run_bass_kernel_spmd(nc, in_maps, core_ids=list(range(NCORE)))
    outs = []
    for b in range(NCORE):
        yo = np.asarray(res.results[b]["yout"])
        xT = yo.reshape(4, 128, 8, 512).transpose(2, 1, 0, 3).reshape(1024, 2048)
        outs.append(xT.T)
    return np.ascontiguousarray(np.stack(outs, axis=0)).astype(np.float32)
